# version 04
# baseline (speedup 1.0000x reference)
import numpy as np
import concourse.bass as bass
import concourse.mybir as mybir
from concourse.bass_utils import run_bass_kernel_spmd

F32 = mybir.dt.float32
BF16 = mybir.dt.bfloat16
AF = mybir.ActivationFunctionType
ALU = mybir.AluOpType
AX = mybir.AxisListType

L = 2048
D = 1024
NDC = 8
NT = 16
H = 16
HD = 64
CAP = 1280.0
KTOP = 1433.0
N_BISECT = 12


class Prog:

    def __init__(self, nc, blk):
        self.nc = nc
        self.blk = blk
        self.cs = {e: [nc.alloc_semaphore("cs_" + e), 0]
                   for e in ("tensor", "vector", "scalar", "gpsimd", "sync")}
        self.ds = {e: [nc.alloc_semaphore("ds_" + e), 0]
                   for e in ("gpsimd", "sync", "scalar")}
        self.dep = None
        self.dep_eng = None

    def comp(self, eng, fn, force_wait=True):
        dep, skip = self.dep, (self.dep_eng == eng and not force_wait)
        sem, cnt = self.cs[eng]

        def body(e):
            if dep is not None and not skip:
                e.wait_ge(dep[0], dep[1])
            fn(e).then_inc(sem, 1)

        getattr(self.blk, eng)(body)
        self.cs[eng][1] = cnt + 1
        self.dep = (sem, cnt + 1)
        self.dep_eng = eng

    def dma(self, eng, fn):
        dep = self.dep
        sem, cnt = self.ds[eng]
        h = {}

        def body(e):
            if dep is not None:
                e.wait_ge(dep[0], dep[1])
            insts = fn(e)
            for i in insts:
                i.then_inc(sem, 16)
            h["n"] = len(insts)

        getattr(self.blk, eng)(body)
        self.ds[eng][1] = cnt + 16 * h["n"]
        self.dep = (sem, self.ds[eng][1])
        self.dep_eng = None

    def finish(self):
        dep = self.dep
        self.blk.sync(lambda e: e.wait_ge(dep[0], dep[1]))



def vseq(p, fns):
    for f in fns:
        p.comp("vector", f)

def _band_small(fn, dtype):
    p = np.arange(128)[:, None]
    q = np.arange(512)[None, :]
    return fn(q - p - 184).astype(dtype)


def host_constants():
    c = {}
    c["c_identity"] = np.eye(128, dtype=np.float32)

    def gp(delta):
        d = np.abs(delta).astype(np.float64)
        return (0.8 * np.exp(-((d - 1.0) ** 2) / 0.5)
                + 0.4 * np.exp(-((d - 30.0) ** 2) / 8.0))
    c["c_gauss"] = _band_small(gp, np.float32)
    c["c_locband"] = _band_small(
        lambda dl: (np.abs(dl) <= 16).astype(np.float64), np.float32)
    inv_freq = 1.0 / (10000.0 ** (np.arange(0, HD, 2, dtype=np.float64) / HD))
    ang = np.arange(L, dtype=np.float64)[:, None] * inv_freq[None, :]
    c["c_rope"] = np.concatenate(
        [np.sin(ang).astype(np.float32).T, np.cos(ang).astype(np.float32).T],
        axis=1).copy()
    c["c_iota8"] = np.tile(np.arange(8, dtype=np.float32), (128, 1))
    c["c_iota8m"] = np.tile(np.arange(8, dtype=np.float32) - 999.0, (128, 1))
    c["c_ones"] = np.ones((128, 128), dtype=np.float32)
    c["c_ones1"] = np.ones((128, 1), dtype=np.float32)
    c["c_onesrow"] = np.ones((1, 128), dtype=np.float32)
    c["c_pow2"] = np.tile(0.5 ** (np.arange(N_BISECT, dtype=np.float64) + 1
                                  ).astype(np.float32), (128, 1))
    return c


def chunk_bias(b):
    n = b.size // 128
    return np.ascontiguousarray(b.reshape(n, 128).T.astype(np.float32))


def head_bias(b):
    return np.ascontiguousarray(b.reshape(16, 64).T.astype(np.float32))


class Ker:
    def __init__(self, nc):
        self.nc = nc
        a = nc.alloc_sbuf_tensor
        self.guard = a("dma_guard", [128, 3072], F32)
        self.x = a("x_res", [128, NDC * L], F32)
        self.xn = a("xn_t", [128, NDC * L], BF16)
        self.pool = a("pool1", [128, NDC * L], BF16)
        self.rope = a("ropebuf", [64, 2 * L], BF16)
        self.outh = a("outh", [64, L], BF16)
        self.wb = a("wbuf", [128, 2048], BF16)
        self.sc32 = a("sc32", [128, L], F32)
        self.scb1 = a("scb1", [128, L], BF16)
        self.scb2 = a("scb2", [128, L], BF16)
        self.scb3 = a("scb3", [128, L], BF16)
        self.vh = a("vhb", [128, NT * HD], BF16)
        self.cons = a("consf", [128, 512], F32)
        self.consb = a("consbb", [32, 2 * L], BF16)
        self.small = a("smallf", [128, 128], F32)
        self.bias = a("biasf", [128, 424], F32)
        self.rbuf = a("rbuff", [128, 200], F32)
        self.hb = a("headbf", [64, 32], F32)
        self.rowa = a("rowaf", [1, 4096], F32)
        self.r8 = a("r8b", [8, 2048], BF16)
        self.ident = a("identb", [128, 128], BF16)
        self.onesf = a("onesf", [128, 1], F32)
        self.onesrow = a("onesrow", [1, 128], F32)
        self.onesbf = a("onesbfc", [128, 1], BF16)
        self.ones8b = a("ones8b", [8, 1], BF16)
        self.e8b = a("e8bb", [8, 8], BF16)
        self.pm_a = nc.alloc_psum_tensor("pm_a", [128, 1024], F32)
        self.pm_b = nc.alloc_psum_tensor("pm_b", [128, 1024], F32)
        self.pm_c = nc.alloc_psum_tensor("pm_c", [128, 512], F32)
        self.pm_d = nc.alloc_psum_tensor("pm_d", [128, 512], F32)
        self.pm_e = nc.alloc_psum_tensor("pm_e", [128, 512], F32)
        self.pm_t = nc.alloc_psum_tensor("pm_t", [128, 1024], BF16)

    def xc(self, dc):
        return self.x[:, dc * L:(dc + 1) * L]

    def xnc(self, dc):
        return self.xn[:, dc * L:(dc + 1) * L]

    def poolc(self, dc):
        return self.pool[:, dc * L:(dc + 1) * L]


def load_globals(p, k, dr):
    def f(e):
        return [e.dma_start(out=k.onesf[:], in_=dr["c_ones1"][:]),
                e.dma_start(out=k.onesrow[:], in_=dr["c_onesrow"][:]),
                e.dma_start(out=k.small[:, 56:64], in_=dr["c_iota8"][:]),
                e.dma_start(out=k.small[:, 80:88], in_=dr["c_iota8m"][:]),
                e.dma_start(out=k.small[:, 88:88 + N_BISECT], in_=dr["c_pow2"][:])]
    p.dma("sync", f)

    def fb(e):
        return [e.dma_start(out=k.ident[:], in_=dr["c_identity"][:]),
                e.dma_start(out=k.ones8b[:], in_=dr["c_ones1"][0:8, :]),
                e.dma_start(out=k.onesbf[:], in_=dr["c_ones1"][:]),
                e.dma_start(out=k.e8b[:], in_=dr["c_identity"][0:8, 0:8])]
    p.dma("gpsimd", fb)

    def fi(e):
        e.memset(k.small[:, 120:128], 1e-5)
        return e.memset(k.small[:, 24:32], 999.0)
    p.comp("vector", fi)


def load_x(p, k, dr, name):
    p.dma("sync", lambda e: [
        e.dma_start(out=k.xc(dc), in_=dr[name][dc * 128:(dc + 1) * 128, :])
        for dc in range(NDC)])


def store_x(p, k, dr, name):
    p.dma("gpsimd", lambda e: [
        e.dma_start(out=dr[name][dc * 128:(dc + 1) * 128, :], in_=k.xc(dc))
        for dc in range(NDC)])


def layernorm(p, k, src=None, dst=None, router=None, src_bf16=False,
              gamma=None, beta=None):
    src = src or k.xc
    dst = dst or k.xnc
    ones = k.onesbf if src_bf16 else k.onesf
    for tb in range(4):
        t0 = tb * 512
        for dc in range(NDC):
            p.comp("scalar", lambda e, dc=dc: e.activation(
                out=k.sc32[:, 0:512], in_=src(dc)[:, t0:t0 + 512],
                func=AF.Square))

            def fmm(e, dc=dc):
                e.matmul(k.pm_c[0:1, 0:512], ones[:, 0:1],
                         src(dc)[:, t0:t0 + 512],
                         start=(dc == 0), stop=(dc == NDC - 1))
                return e.matmul(k.pm_d[0:1, 0:512], k.onesf[:, 0:1],
                                k.sc32[:, 0:512],
                                start=(dc == 0), stop=(dc == NDC - 1))
            p.comp("tensor", fmm)

        vseq(p, [
            lambda e: (e.tensor_scalar_mul(k.rowa[0:1, 0:512],
                                           k.pm_c[0:1, 0:512], 1.0 / D),
                       e.tensor_scalar_mul(k.rowa[0:1, 512:1024],
                                           k.pm_d[0:1, 0:512], 1.0 / D))[-1],
            lambda e: e.tensor_mul(k.rowa[0:1, 1024:1536], k.rowa[0:1, 0:512],
                                   k.rowa[0:1, 0:512]),
            lambda e: e.tensor_sub(k.rowa[0:1, 512:1024],
                                   k.rowa[0:1, 512:1024],
                                   k.rowa[0:1, 1024:1536])])
        p.comp("scalar", lambda e: e.activation(
            out=k.rowa[0:1, 512:1024], in_=k.rowa[0:1, 512:1024], func=AF.Sqrt,
            bias=k.small[0:1, 127:128]))
        p.comp("vector", lambda e: e.reciprocal(k.rowa[0:1, 512:1024],
                                                k.rowa[0:1, 512:1024]))

        def fbc(e):
            e.matmul(k.pm_c[:, 0:512], k.onesrow[:], k.rowa[0:1, 0:512],
                     start=True, stop=True)
            return e.matmul(k.pm_d[:, 0:512], k.onesrow[:],
                            k.rowa[0:1, 512:1024], start=True, stop=True)
        p.comp("tensor", fbc)

        def fev(e):
            e.tensor_copy(out=k.sc32[:, 512:1024], in_=k.pm_c[:, 0:512])
            return e.tensor_copy(out=k.sc32[:, 1024:1536], in_=k.pm_d[:, 0:512])
        p.comp("vector", fev)

        for dc in range(NDC):
            if src_bf16:
                p.comp("vector", lambda e, dc=dc: e.tensor_copy(
                    out=k.sc32[:, 0:512], in_=src(dc)[:, t0:t0 + 512]))
                p.comp("vector", lambda e: e.tensor_sub(
                    k.sc32[:, 0:512], k.sc32[:, 0:512], k.sc32[:, 512:1024]))
            else:
                p.comp("vector", lambda e, dc=dc: e.tensor_sub(
                    k.sc32[:, 0:512], src(dc)[:, t0:t0 + 512],
                    k.sc32[:, 512:1024]))
            p.comp("vector", lambda e: e.tensor_mul(
                k.sc32[:, 0:512], k.sc32[:, 0:512], k.sc32[:, 1024:1536]))
            p.comp("vector", lambda e, dc=dc: e.tensor_copy(
                out=dst(dc)[:, t0:t0 + 512], in_=k.sc32[:, 0:512]))
            if gamma is not None:
                p.comp("vector", lambda e, dc=dc: e.tensor_scalar(
                    dst(dc)[:, t0:t0 + 512], dst(dc)[:, t0:t0 + 512],
                    gamma(dc), beta(dc), ALU.mult, ALU.add))
            if router is not None:
                def frmm(e, dc=dc):
                    last = None
                    for sub in range(4):
                        last = e.matmul(
                            k.pm_e[:, sub * 8:(sub + 1) * 8],
                            k.sc32[:, sub * 128:(sub + 1) * 128],
                            router["w"][:, dc * 8:(dc + 1) * 8],
                            start=(dc == 0 and sub == 0),
                            stop=(dc == NDC - 1 and sub == 3))
                    return last
                p.comp("tensor", frmm)
        if router is not None:
            def flev(e):
                last = None
                for sub in range(4):
                    tt = tb * 4 + sub
                    last = e.tensor_add(
                        router["out"][:, tt * 8:(tt + 1) * 8],
                        k.pm_e[:, sub * 8:(sub + 1) * 8], router["bias"])
                return last
            p.comp("vector", flev)


def argmax_to_ohT(p, k, logits, ohT):
    for tt in range(NT):
        sl = logits[:, tt * 8:(tt + 1) * 8]
        vseq(p, [
            lambda e, sl=sl: e.reduce_max(k.small[:, 0:1], sl, axis=AX.X),
            lambda e, sl=sl: e.tensor_scalar(k.small[:, 8:16], sl,
                                             k.small[:, 0:1], None, ALU.is_ge),
            lambda e: e.tensor_mul(k.small[:, 16:24], k.small[:, 8:16],
                                   k.small[:, 80:88]),
            lambda e: e.tensor_scalar(k.small[:, 16:24], k.small[:, 16:24],
                                      999.0, None, ALU.add),
            lambda e: e.tensor_reduce(k.small[:, 1:2], k.small[:, 16:24],
                                      axis=AX.X, op=ALU.min),
            lambda e, tt=tt: e.tensor_scalar(k.scb1[:, tt * 8:(tt + 1) * 8],
                                             k.small[:, 56:64],
                                             k.small[:, 1:2], None,
                                             ALU.is_equal)])
    for tt in range(NT):
        p.comp("tensor", lambda e, tt=tt: e.transpose(
            k.pm_t[0:8, 0:128], k.scb1[:, tt * 8:(tt + 1) * 8], k.ident[:]))
        p.comp("vector", lambda e, tt=tt: e.tensor_copy(
            out=ohT[0:8, tt * 128:(tt + 1) * 128], in_=k.pm_t[0:8, 0:128]))


def wload(p, k, dram_ap, cols, dst_off, kchunks):
    c0, c1 = cols
    w = c1 - c0

    def f(e):
        src = dram_ap[:, c0:c1].rearrange("(kc p) c -> p kc c", p=128)
        dst = k.wb[:, dst_off:dst_off + kchunks * w].rearrange(
            "p (kc c) -> p kc c", c=w)
        return [e.dma_start(out=dst, in_=src)]
    p.dma("gpsimd", f)




def mm_streamed(p, k, w_ap, cols, nkc, rhs_fn, n0, n1, pm, mrows=128,
                rhs_abs=False):
    c0, c1 = cols
    w = c1 - c0
    per = max(1, 2048 // w)
    for g0 in range(0, nkc, per):
        g1 = min(nkc, g0 + per)
        srcap = w_ap[g0 * 128:g1 * 128, c0:c1].rearrange(
            "(kc p) c -> p kc c", p=128)
        p.dma("gpsimd", lambda e, srcap=srcap, g0=g0, g1=g1: [e.dma_start(
            out=k.wb[:, 0:(g1 - g0) * w].rearrange("p (kc c) -> p kc c", c=w),
            in_=srcap)])

        def fmm(e, g0=g0, g1=g1):
            last = None
            for kc in range(g0, g1):
                r = rhs_fn(kc) if rhs_abs else rhs_fn(kc)[:, n0:n1]
                for s0 in range(0, n1 - n0, 512):
                    s1 = min(n1 - n0, s0 + 512)
                    last = e.matmul(pm[0:mrows, s0:s1],
                                    k.wb[:, (kc - g0) * w:(kc - g0 + 1) * w],
                                    r[:, s0:s1],
                                    start=(kc == 0), stop=(kc == nkc - 1))
            return last
        p.comp("tensor", fmm)

def linear_oc(p, k, rhs_fn, wtile_fn, oc, nkc, n0, n1, pm, mrows=128):
    def f(e):
        last = None
        for kc in range(nkc):
            for s0 in range(0, n1 - n0, 512):
                s1 = min(n1 - n0, s0 + 512)
                last = e.matmul(pm[0:mrows, s0:s1], wtile_fn(kc),
                                rhs_fn(kc)[:, n0 + s0:n0 + s1],
                                start=(kc == 0), stop=(kc == nkc - 1))
        return last
    p.comp("tensor", f)


def attn_proj_s(p, k, attnout, w_dram, bias_sb, s_dram, pool_ok=True):
    for dc in range(NDC):
        p.dma("gpsimd", lambda e, dc=dc: [e.dma_start(
            out=k.pool[:, dc * L:(dc + 1) * L],
            in_=attnout[dc * 128:(dc + 1) * 128, :])])
    for oc in range(NDC):
        wload(p, k, w_dram, (oc * 128, (oc + 1) * 128), 0, 8)
        for tbb in range(2):
            linear_oc(p, k, lambda kc: k.pool[:, kc * L:(kc + 1) * L],
                      lambda kc: k.wb[:, kc * 128:(kc + 1) * 128],
                      oc, NDC, tbb * 1024, (tbb + 1) * 1024, k.pm_a)
            p.comp("vector", lambda e, oc=oc: e.tensor_scalar(
                k.scb1[:, 0:1024], k.pm_a[:, 0:1024],
                bias_sb[:, oc:oc + 1], None, ALU.add))
            p.dma("gpsimd", lambda e, oc=oc, tbb=tbb: [e.dma_start(
                out=s_dram[oc * 128:(oc + 1) * 128,
                           tbb * 1024:(tbb + 1) * 1024],
                in_=k.scb1[:, 0:1024])])


def softmax_to_AT(p, k):
    p.comp("vector", lambda e: e.reduce_max(k.small[:, 0:1], k.sc32[:],
                                            axis=AX.X, negate=True))
    p.comp("scalar", lambda e: e.activation(
        out=k.scb1[:], in_=k.sc32[:], func=AF.Exp, bias=k.small[:, 0:1],
        accum_out=k.small[:, 1:2]))

    p.comp("vector", lambda e: e.reciprocal(k.small[:, 1:2],
                                            k.small[:, 1:2]))
    p.comp("vector", lambda e: e.tensor_scalar(k.scb1[:], k.scb1[:],
                                               k.small[:, 1:2], None,
                                               ALU.mult))
    transpose_AT(p, k)


def transpose_AT(p, k):
    for r in range(2):
        def ftr(e, r=r):
            last = None
            for j in range(8):
                jc = r * 8 + j
                last = e.transpose(k.pm_t[:, j * 128:(j + 1) * 128],
                                   k.scb1[:, jc * 128:(jc + 1) * 128],
                                   k.ident[:])
            return last
        p.comp("tensor", ftr)
        p.comp("vector", lambda e, r=r: e.tensor_copy(
            out=k.scb3[:, r * 1024:(r + 1) * 1024], in_=k.pm_t[:, 0:1024]))


def attn_av(p, k, it, h, attnout):
    def f(e):
        last = None
        for jc in range(16):
            last = e.matmul(k.pm_c[0:64, 0:128], k.vh[:, jc * 64:(jc + 1) * 64],
                            k.scb3[:, jc * 128:(jc + 1) * 128],
                            start=(jc == 0), stop=(jc == 15))
        return last
    p.comp("tensor", f)
    p.comp("vector", lambda e: e.tensor_copy(
        out=k.outh[:, it * 128:(it + 1) * 128], in_=k.pm_c[0:64, 0:128]))


def store_outh(p, k, h, attnout):
    p.dma("gpsimd", lambda e: [e.dma_start(
        out=attnout[h * 64:(h + 1) * 64, :], in_=k.outh[:, :])])


def compute_vh(p, k, src_fn, wv_off, vbias=None):
    for tt in range(NT):
        def f(e, tt=tt):
            last = None
            for dc in range(NDC):
                last = e.matmul(k.pm_c[:, 0:64],
                                src_fn(dc)[:, tt * 128:(tt + 1) * 128],
                                k.wb[:, wv_off + dc * 64:wv_off + (dc + 1) * 64],
                                start=(dc == 0), stop=(dc == NDC - 1))
            return last
        p.comp("tensor", f)
        if vbias is None:
            p.comp("vector", lambda e, tt=tt: e.tensor_copy(
                out=k.vh[:, tt * 64:(tt + 1) * 64], in_=k.pm_c[:, 0:64]))
        else:
            p.comp("vector", lambda e, tt=tt: e.tensor_add(
                k.vh[:, tt * 64:(tt + 1) * 64], k.pm_c[:, 0:64], vbias))


def qk_head(p, k, src_fn, w_off, dst_off, bias_col=None, scale=None):
    for tbb in range(2):
        def f(e, tbb=tbb):
            last = None
            for dc in range(NDC):
                for s0 in (0, 512):
                    last = e.matmul(
                        k.pm_a[0:64, s0:s0 + 512],
                        k.wb[:, w_off + dc * 64:w_off + (dc + 1) * 64],
                        src_fn(dc)[:, tbb * 1024 + s0:tbb * 1024 + s0 + 512],
                        start=(dc == 0), stop=(dc == NDC - 1))
            return last
        p.comp("tensor", f)

        def fe(e, tbb=tbb):
            dst = k.rope[0:64, dst_off + tbb * 1024:dst_off + (tbb + 1) * 1024]
            if bias_col is not None:
                return e.tensor_scalar(dst, k.pm_a[0:64, 0:1024], bias_col,
                                       None, ALU.add)
            return e.tensor_copy(out=dst, in_=k.pm_a[0:64, 0:1024])
        p.comp("vector", fe)


def qk_head_rope(p, k, src_fn, w_off, dst_off):
    for tbb in range(2):
        def f(e, tbb=tbb):
            last = None
            for dc in range(NDC):
                for s0 in (0, 512):
                    last = e.matmul(
                        k.pm_a[0:64, s0:s0 + 512],
                        k.wb[:, w_off + dc * 64:w_off + (dc + 1) * 64],
                        src_fn(dc)[:, tbb * 1024 + s0:tbb * 1024 + s0 + 512],
                        start=(dc == 0), stop=(dc == NDC - 1))
            return last
        p.comp("tensor", f)

        c0 = tbb * 1024
        sin = k.consb[:, c0:c0 + 1024]
        cos = k.consb[:, L + c0:L + c0 + 1024]
        d0 = dst_off + c0
        vseq(p, [
            lambda e: (e.tensor_copy(out=k.scb1[0:32, 0:1024],
                                     in_=k.pm_a[0:32, 0:1024]),
                       e.tensor_copy(out=k.scb1[0:32, 1024:2048],
                                     in_=k.pm_a[32:64, 0:1024]))[-1],
            lambda e: (e.tensor_mul(k.sc32[0:32, 0:1024],
                                    k.scb1[0:32, 0:1024], cos),
                       e.tensor_mul(k.sc32[0:32, 1024:2048],
                                    k.scb1[0:32, 1024:2048], sin))[-1],
            lambda e: e.tensor_sub(k.rope[0:32, d0:d0 + 1024],
                                   k.sc32[0:32, 0:1024],
                                   k.sc32[0:32, 1024:2048]),
            lambda e: (e.tensor_mul(k.sc32[0:32, 0:1024],
                                    k.scb1[0:32, 1024:2048], cos),
                       e.tensor_mul(k.sc32[0:32, 1024:2048],
                                    k.scb1[0:32, 0:1024], sin))[-1],
            lambda e: e.tensor_add(k.rope[32:64, d0:d0 + 1024],
                                   k.sc32[0:32, 0:1024],
                                   k.sc32[0:32, 1024:2048])])


def scores_tile(p, k, it):
    def f(e):
        last = None
        for s0 in (0, 512, 1024, 1536):
            pm = k.pm_a if s0 < 1024 else k.pm_b
            last = e.matmul(pm[:, s0 % 1024:s0 % 1024 + 512],
                            k.rope[0:64, it * 128:(it + 1) * 128],
                            k.rope[0:64, L + s0:L + s0 + 512],
                            start=True, stop=True)
        return last
    p.comp("tensor", f)


def band_cols(it):
    c0 = max(0, it * 128 - 184)
    c1 = min(L, it * 128 + 312)
    q0 = c0 - it * 128 + 184
    return c0, c1, q0


def attention_l1(p, k, dr, attnout):
    for h in range(H):
        c = (h * 64, (h + 1) * 64)
        wload(p, k, dr["L1_attn_WqT"], c, 0, 8)
        wload(p, k, dr["L1_attn_WkT"], c, 512, 8)
        wload(p, k, dr["L1_attn_WvT"], c, 1024, 8)
        p.dma("sync", lambda e, h=h: [e.dma_start(
            out=k.bias[:, 80:144], in_=dr["L1_attn_bvt"][:, h * 64:(h + 1) * 64])])
        qk_head(p, k, k.xnc, 0, 0, bias_col=k.hb[:, h:h + 1])
        qk_head(p, k, k.xnc, 512, L, bias_col=k.hb[:, 16 + h:16 + h + 1])
        compute_vh(p, k, k.xnc, 1024, vbias=k.bias[:, 80:144])
        for it in range(NT):
            scores_tile(p, k, it)
            c0, c1, q0 = band_cols(it)

            def fev(e, it=it, c0=c0, c1=c1, q0=q0):
                e.tensor_scalar_mul(k.sc32[:, 0:1024], k.pm_a[:, 0:1024], 0.125)
                e.tensor_scalar_mul(k.sc32[:, 1024:2048], k.pm_b[:, 0:1024],
                                    0.125)
                return e.scalar_tensor_tensor(
                    k.sc32[:, c0:c1], k.cons[:, q0:q0 + (c1 - c0)],
                    k.bias[:, 62:63], k.sc32[:, c0:c1], ALU.mult, ALU.add)
            p.comp("vector", fev)
            softmax_to_AT(p, k)
            attn_av(p, k, it, h, attnout)
        store_outh(p, k, h, attnout)


def attention_l3(p, k, dr, attnout, xn3):
    for h in range(H):
        wload(p, k, dr["L3_attn_w_qkv"], (h * 64, (h + 1) * 64), 0, 8)
        wload(p, k, dr["L3_attn_w_qkv"], (1024 + h * 64, 1024 + (h + 1) * 64),
              512, 8)
        wload(p, k, dr["L3_attn_w_qkv"], (2048 + h * 64, 2048 + (h + 1) * 64),
              1024, 8)
        qk_head_rope(p, k, xn3, 0, 0)
        qk_head_rope(p, k, xn3, 512, L)
        compute_vh(p, k, xn3, 1024)
        for it in range(NT):
            scores_tile(p, k, it)

            def fev(e, it=it):
                e.tensor_scalar_mul(k.sc32[:, 0:1024], k.pm_a[:, 0:1024], 0.125)
                return e.tensor_scalar_mul(k.sc32[:, 1024:2048],
                                           k.pm_b[:, 0:1024], 0.125)
            p.comp("vector", fev)
            softmax_to_AT(p, k)
            attn_av(p, k, it, h, attnout)
        store_outh(p, k, h, attnout)


def attention_l0(p, k, dr, attnout):
    for h in range(H):
        wload(p, k, dr["L0_attn_qkv_wT"], (h * 64, (h + 1) * 64), 0, 8)
        wload(p, k, dr["L0_attn_qkv_wT"], (1024 + h * 64, 1024 + (h + 1) * 64),
              512, 8)
        wload(p, k, dr["L0_attn_qkv_wT"], (2048 + h * 64, 2048 + (h + 1) * 64),
              1024, 8)
        qk_head(p, k, k.xnc, 0, 0)
        qk_head(p, k, k.xnc, 512, L)
        compute_vh(p, k, k.xnc, 1024)
        def fsp(e, h=h):
            e.tensor_copy(out=k.small[:, 6:7], in_=k.bias[:, 32 + h:33 + h])
            return e.tensor_copy(out=k.small[:, 7:8],
                                 in_=k.bias[:, 48 + h:49 + h])
        p.comp("vector", fsp)
        for it in range(NT):
            scores_tile(p, k, it)

            def fev(e, it=it):
                e.tensor_scalar_mul(k.sc32[:, 0:1024], k.pm_a[:, 0:1024], 0.125)
                e.tensor_scalar_mul(k.sc32[:, 1024:2048], k.pm_b[:, 0:1024],
                                    0.125)
                return e.tensor_scalar(k.scb1[:], k.sc32[:],
                                       k.small[:, 6:7], k.small[:, 7:8],
                                       ALU.mult, ALU.add)
            p.comp("vector", fev)

            def fbis(e, it=it):
                lo = k.small[:, 9:10]
                rng = k.small[:, 10:11]
                cnt = k.small[:, 11:12]
                ge = k.small[:, 12:13]
                e.tensor_reduce(lo, k.scb1[:], axis=AX.X, op=ALU.min)
                e.reduce_max(rng, k.scb1[:], axis=AX.X)
                e.tensor_sub(rng, rng, lo)
                e.tensor_scalar(k.small[:, 100:100 + N_BISECT], k.small[:, 88:88 + N_BISECT], rng, None, ALU.mult)
                thr = k.small[:, 13:14]
                last = None
                for i in range(N_BISECT):
                    w = k.small[:, 100 + i:101 + i]
                    e.tensor_scalar(thr, lo, w, None, ALU.add)
                    e.tensor_scalar(k.scb2[:], k.scb1[:], thr, None,
                                    ALU.is_ge, ALU.add, accum_out=cnt)
                    e.tensor_scalar(ge, cnt, KTOP, None, ALU.is_ge)
                    last = e.scalar_tensor_tensor(lo, ge, w, lo,
                                                  ALU.mult, ALU.add)
                return last
            p.comp("vector", fbis)

            c0, c1, q0 = band_cols(it)

            vseq(p, [
                lambda e: e.tensor_scalar(k.scb2[:], k.scb1[:],
                                          k.small[:, 9:10], k.small[:, 5:6],
                                          ALU.is_ge, ALU.mult),
                lambda e: e.tensor_scalar(k.scb3[:], k.scb2[:],
                                          k.small[:, 4:5], None, ALU.is_gt),
                lambda e, c0=c0, c1=c1, q0=q0: e.scalar_tensor_tensor(
                    k.scb2[:, c0:c1], k.cons[:, q0:q0 + (c1 - c0)],
                    k.small[:, 3:4], k.scb2[:, c0:c1], ALU.mult, ALU.add),
                lambda e, c0=c0, c1=c1: e.tensor_scalar(
                    k.scb3[:, c0:c1], k.scb2[:, c0:c1], k.small[:, 4:5], None,
                    ALU.is_gt),
                lambda e: e.reduce_max(k.small[:, 0:1], k.sc32[:], axis=AX.X,
                                       negate=True)])
            p.comp("scalar", lambda e: e.activation(
                out=k.scb2[:], in_=k.sc32[:], func=AF.Exp,
                bias=k.small[:, 0:1]))

            vseq(p, [
                lambda e: e.tensor_mul(k.scb2[:], k.scb2[:], k.scb3[:]),
                lambda e: e.reduce_sum(k.small[:, 1:2], k.scb3[:], axis=AX.X),
                lambda e: e.tensor_scalar(k.small[:, 12:13], k.small[:, 1:2],
                                          0.5, None, ALU.is_lt),
                lambda e: e.scalar_tensor_tensor(k.scb2[:, 0:1],
                                                 k.small[:, 12:13], 1.0,
                                                 k.scb2[:, 0:1], ALU.mult,
                                                 ALU.max),
                lambda e: e.reduce_sum(k.small[:, 1:2], k.scb2[:], axis=AX.X),
                lambda e: e.reciprocal(k.small[:, 1:2], k.small[:, 1:2]),
                lambda e: e.tensor_scalar(k.scb1[:], k.scb2[:],
                                          k.small[:, 1:2], None, ALU.mult)])
            transpose_AT(p, k)
            attn_av(p, k, it, h, attnout)
        store_outh(p, k, h, attnout)


def pattern_path(p, k, dr):
    A, B, C, Dh = 0, 1024, 2048, 3072

    def fpool(e):
        last = None
        for dc in range(NDC):
            e.reduce_sum(k.small[:, 32 + dc:33 + dc], k.xnc(dc), axis=AX.X)
            last = e.tensor_scalar_mul(k.small[:, 32 + dc:33 + dc],
                                       k.small[:, 32 + dc:33 + dc], 1.0 / L)
        return last
    p.comp("vector", fpool)
    for dc in range(NDC):
        p.comp("scalar", lambda e, dc=dc: e.activation(
            out=k.sc32[:], in_=k.xnc(dc), func=AF.Square))
        p.comp("vector", lambda e, dc=dc: e.reduce_sum(
            k.small[:, 40 + dc:41 + dc], k.sc32[:], axis=AX.X))

    vseq(p, [
        lambda e: e.tensor_mul(k.small[:, 48:56], k.small[:, 32:40],
                               k.small[:, 32:40]),
        lambda e: e.tensor_scalar(k.small[:, 48:56], k.small[:, 48:56],
                                  -float(L), None, ALU.mult),
        lambda e: e.tensor_add(k.small[:, 48:56], k.small[:, 48:56],
                               k.small[:, 40:48]),
        lambda e: e.tensor_scalar(k.small[:, 48:56], k.small[:, 48:56],
                                  1.0 / (L - 1), None, ALU.mult),
        lambda e: e.tensor_copy(out=k.scb1[:, 0:8], in_=k.small[:, 32:40])])

    p.comp("tensor", lambda e: e.matmul(k.pm_e[0:1, 0:8], k.onesf[:, 0:1],
                                        k.small[:, 48:56], start=True,
                                        stop=True))

    vseq(p, [
        lambda e: e.reduce_sum(k.rowa[0:1, 3600:3601], k.pm_e[0:1, 0:8],
                               axis=AX.X),
        lambda e: e.tensor_scalar_mul(k.rowa[0:1, 3600:3601],
                                      k.rowa[0:1, 3600:3601], 1.0 / D),
        lambda e: (e.memset(k.scb1[:, 8:10], 0.0),
                   e.memset(k.scb1[0:1, 8:9], 4.0))[-1],
        lambda e: e.tensor_copy(out=k.scb1[0:1, 9:10],
                                in_=k.rowa[0:1, 3600:3601])])

    p.dma("sync", lambda e: [e.dma_start(out=k.rowa[0:1, B:B + 1024],
                                         in_=dr["L0_featb_r"][:])])
    for nh in range(2):
        p.dma("gpsimd", lambda e, nh=nh: [
            e.dma_start(out=k.scb3[0:1, 0:512],
                        in_=dr["L0_attn_featT"][1024:1025,
                                                nh * 512:(nh + 1) * 512]),
            e.dma_start(out=k.scb3[0:1, 512:1024],
                        in_=dr["L0_attn_featT"][1152:1153,
                                                nh * 512:(nh + 1) * 512])])
        for g in range(2):
            wload(p, k, dr["L0_attn_featT"][g * 512:(g + 1) * 512, :],
                  (nh * 512, (nh + 1) * 512), 0, 4)

            def fmm(e, nh=nh, g=g):
                last = None
                for kc in range(4 * g, 4 * g + 4):
                    last = e.matmul(k.pm_a[0:1, 0:512],
                                    k.scb1[:, kc:kc + 1],
                                    k.wb[:, (kc - 4 * g) * 512:
                                         (kc - 4 * g + 1) * 512],
                                    start=(kc == 0), stop=False)
                return last
            p.comp("tensor", fmm)

        def fmm2(e, nh=nh):
            e.matmul(k.pm_a[0:1, 0:512], k.scb1[0:1, 8:9],
                     k.scb3[0:1, 0:512], start=False, stop=False)
            return e.matmul(k.pm_a[0:1, 0:512], k.scb1[0:1, 9:10],
                            k.scb3[0:1, 512:1024], start=False, stop=True)
        p.comp("tensor", fmm2)
        p.comp("vector", lambda e, nh=nh: e.tensor_add(
            k.rowa[0:1, A + nh * 512:A + (nh + 1) * 512],
            k.pm_a[0:1, 0:512], k.rowa[0:1, B + nh * 512:B + (nh + 1) * 512]))

    def mlp_stage(src0, n_in, n_out, wname, bname, dst0, relu=True):
        nic = (n_in + 127) // 128
        p.dma("sync", lambda e: [e.dma_start(
            out=k.rowa[0:1, B:B + n_out], in_=dr[bname][:])])
        p.comp("vector", lambda e: e.tensor_copy(
            out=k.scb2[0:1, 0:n_in], in_=k.rowa[0:1, src0:src0 + n_in]))
        for ic in range(nic):
            p.comp("tensor", lambda e, ic=ic: e.transpose(
                k.pm_t[:, 0:1], k.scb2[0:1, ic * 128:(ic + 1) * 128],
                k.ident[0:1, 0:1]))
            p.comp("vector", lambda e, ic=ic: e.tensor_copy(
                out=k.scb1[:, 16 + ic:17 + ic], in_=k.pm_t[:, 0:1]))
        for nh in range((n_out + 511) // 512):
            n0 = nh * 512
            n1 = min(n_out, n0 + 512)
            per = max(1, 2048 // (n1 - n0))
            for g0 in range(0, nic, per):
                g1 = min(nic, g0 + per)
                wload(p, k, dr[wname][g0 * 128:g1 * 128, :], (n0, n1), 0,
                      g1 - g0)

                def fmm(e, n0=n0, n1=n1, g0=g0, g1=g1, nic=nic):
                    last = None
                    for kc in range(g0, g1):
                        last = e.matmul(
                            k.pm_a[0:1, 0:n1 - n0],
                            k.scb1[:, 16 + kc:17 + kc],
                            k.wb[:, (kc - g0) * (n1 - n0):
                                 (kc - g0 + 1) * (n1 - n0)],
                            start=(kc == 0), stop=(kc == nic - 1))
                    return last
                p.comp("tensor", fmm)

            p.comp("vector", lambda e, n0=n0, n1=n1: e.tensor_add(
                k.rowa[0:1, dst0 + n0:dst0 + n1], k.pm_a[0:1, 0:n1 - n0],
                k.rowa[0:1, B + n0:B + n1]))
            if relu:
                p.comp("vector", lambda e, n0=n0, n1=n1: e.tensor_scalar_max(
                    k.rowa[0:1, dst0 + n0:dst0 + n1],
                    k.rowa[0:1, dst0 + n0:dst0 + n1], 0.0))

    mlp_stage(A, 1024, 1024, "L0_attn_ps1T", "L0_ps1b_r", C)
    mlp_stage(C, 1024, 512, "L0_attn_ps2T", "L0_ps2b_r", Dh)
    mlp_stage(Dh, 512, 3, "L0_attn_ps3T", "L0_ps3b_r", 3584, relu=False)
    p.dma("sync", lambda e: [e.dma_start(out=k.rowa[0:1, B:B + 3],
                                         in_=dr["L0_patb_r"][:])])

    lg = k.rowa[0:1, 3584:3587]
    vseq(p, [
        lambda e: e.tensor_add(lg, lg, k.rowa[0:1, B:B + 3]),
        lambda e: e.tensor_scalar_mul(lg, lg, 1.0 / 0.3),
        lambda e: e.reduce_max(k.rowa[0:1, 3600:3601], lg, axis=AX.X,
                               negate=True)])
    p.comp("scalar", lambda e: e.activation(
        out=k.rowa[0:1, 3584:3587], in_=k.rowa[0:1, 3584:3587], func=AF.Exp,
        bias=k.rowa[0:1, 3600:3601], accum_out=k.rowa[0:1, 3601:3602]))

    vseq(p, [
        lambda e: e.reciprocal(k.rowa[0:1, 3601:3602], k.rowa[0:1, 3601:3602]),
        lambda e: e.tensor_scalar(k.rowa[0:1, 3584:3587],
                                  k.rowa[0:1, 3584:3587],
                                  k.rowa[0:1, 3601:3602], None, ALU.mult),
        lambda e: e.tensor_scalar(k.rowa[0:1, 3604:3605],
                                  k.rowa[0:1, 3585:3586],
                                  -1.0, 0.02, ALU.mult, ALU.add)])

    def fbc(e):
        e.matmul(k.pm_e[:, 0:1], k.onesrow[:], k.rowa[0:1, 3584:3585],
                 start=True, stop=True)
        e.matmul(k.pm_e[:, 1:2], k.onesrow[:], k.rowa[0:1, 3604:3605],
                 start=True, stop=True)
        return e.matmul(k.pm_e[:, 2:3], k.onesrow[:], k.rowa[0:1, 3586:3587],
                        start=True, stop=True)
    p.comp("tensor", fbc)
    p.comp("vector", lambda e: e.tensor_copy(out=k.small[:, 3:6],
                                             in_=k.pm_e[:, 0:3]))


def build_masks_l0(p, k, dr):
    p.dma("gpsimd", lambda e: [e.dma_start(out=k.r8[0:8, 0:2048],
                                           in_=dr["r_ohT"][:, :])])
    p.dma("sync", lambda e: [e.dma_start(out=k.sc32[0:8, 0:2048], in_=dr["r_cum"][:]),
                             e.dma_start(out=k.small[0:8, 64:72],
                                         in_=dr["counts_prior"][:])])

    vseq(p, [
        lambda e: e.reduce_sum(k.small[0:8, 72:73], k.small[0:8, 64:72],
                               axis=AX.X),
        lambda e: e.tensor_scalar(k.scb1[0:8, 0:2048], k.sc32[0:8, 0:2048],
                                  k.small[0:8, 72:73], 1281.0, ALU.add,
                                  ALU.is_lt),
        lambda e: e.tensor_mul(k.r8[0:8, 0:2048], k.r8[0:8, 0:2048],
                               k.scb1[0:8, 0:2048])])

    def fm(e):
        last = None
        for s0 in (0, 512, 1024, 1536):
            pm = k.pm_a if s0 < 1024 else k.pm_b
            last = e.matmul(pm[0:1, s0 % 1024:s0 % 1024 + 512],
                            k.ones8b[0:8, 0:1], k.r8[0:8, s0:s0 + 512],
                            start=True, stop=True)
        return last
    p.comp("tensor", fm)

    vseq(p, [
        lambda e: (e.tensor_copy(out=k.rowa[0:1, 0:1024],
                                 in_=k.pm_a[0:1, 0:1024]),
                   e.tensor_copy(out=k.rowa[0:1, 1024:2048],
                                 in_=k.pm_b[0:1, 0:1024]))[-1],
        lambda e: e.tensor_scalar(k.rowa[0:1, 2048:4096], k.rowa[0:1, 0:2048],
                                  -1.0, 1.0, ALU.mult, ALU.add)])


def moe_ffn(p, k, dr, layer, s_dram):
    if layer == 0:
        w1n, b1off, w2n, b2off = "L0_ffn_w1", 64, "L0_ffn_w2", 320
        actf = AF.Relu
    else:
        w1n, b1off, w2n, b2off = "L1_ffn_W1", 64, "L1_ffn_W2", 320
        actf = AF.Gelu
    n_exp = 9 if layer == 0 else 8
    for tb in range(4):
        t0 = tb * 512

        def fz(e):
            e.memset(k.scb1[:], 0.0)
            return e.memset(k.scb2[:], 0.0)
        p.comp("vector", fz)
        for ex in range(n_exp):
            ovf = (layer == 0 and ex == 8)
            if ovf:
                p.comp("vector", lambda e, t0=t0: e.tensor_copy(
                    out=k.sc32[0:1, 1536:2048],
                    in_=k.rowa[0:1, 2048 + t0:2048 + t0 + 512]))
            else:
                p.comp("tensor", lambda e, ex=ex, t0=t0: e.matmul(
                    k.pm_e[0:1, 0:512], k.e8b[0:8, ex:ex + 1],
                    k.r8[0:8, t0:t0 + 512], start=True, stop=True))
                p.comp("vector", lambda e: e.tensor_copy(
                    out=k.sc32[0:1, 1536:2048], in_=k.pm_e[0:1, 0:512]))
            p.comp("tensor", lambda e: e.matmul(
                k.pm_e[:, 0:512], k.onesrow[:], k.sc32[0:1, 1536:2048],
                start=True, stop=True))
            p.comp("vector", lambda e: e.tensor_copy(out=k.scb3[:, 0:512],
                                                     in_=k.pm_e[:, 0:512]))
            if ovf:
                w1ap, w2ap = dr["L0_ffn_ow1"], dr["L0_ffn_ow2"]
            else:
                w1ap = dr[w1n][ex * 1024:(ex + 1) * 1024, :]
                w2ap = dr[w2n][ex * 4096:(ex + 1) * 4096, :]
            for hc in range(32):
                wload(p, k, w1ap, (hc * 128, (hc + 1) * 128), 0, 8)
                linear_oc(p, k, k.xnc,
                          lambda kc: k.wb[:, kc * 128:(kc + 1) * 128],
                          hc, 8, t0, t0 + 512, k.pm_c)
                if ovf:
                    bcol = k.bias[:, 384 + hc:385 + hc]
                else:
                    bcol = k.bias[:, b1off + hc * 8 + ex:b1off + hc * 8 + ex + 1]
                p.comp("scalar", lambda e, hc=hc, bcol=bcol: e.activation(
                    out=k.pool[:, hc * 512:(hc + 1) * 512],
                    in_=k.pm_c[:, 0:512], func=actf, bias=bcol))
            for oc in range(8):
                mm_streamed(p, k, w2ap, (oc * 128, (oc + 1) * 128), 32,
                            lambda kc: k.pool[:, kc * 512:(kc + 1) * 512],
                            0, 512, k.pm_c)
                if ovf:
                    b2col = k.bias[:, 416 + oc:417 + oc]
                else:
                    b2col = k.bias[:, b2off + oc * 8 + ex:b2off + oc * 8 + ex + 1]
                p.comp("vector", lambda e, b2col=b2col: e.tensor_scalar(
                    k.vh[:, 0:512], k.pm_c[:, 0:512], b2col, None, ALU.add))
                gsl = (k.scb1 if oc < 4 else k.scb2)[:, (oc % 4) * 512:
                                                     (oc % 4 + 1) * 512]

                p.comp("vector", lambda e: e.tensor_mul(
                    k.vh[:, 512:1024], k.vh[:, 0:512], k.scb3[:, 0:512]))
                p.comp("vector", lambda e, gsl=gsl: e.tensor_add(
                    gsl, gsl, k.vh[:, 512:1024]))
        for oc in range(8):
            gsl = (k.scb1 if oc < 4 else k.scb2)[:, (oc % 4) * 512:
                                                 (oc % 4 + 1) * 512]
            p.dma("sync", lambda e, oc=oc, t0=t0: [e.dma_start(
                out=k.scb3[:, 512:1024],
                in_=s_dram[oc * 128:(oc + 1) * 128, t0:t0 + 512])])
            p.comp("scalar", lambda e, gsl=gsl: e.activation(
                out=k.vh[:, 0:512], in_=gsl, func=AF.Silu))

            p.comp("vector", lambda e: e.tensor_mul(
                k.sc32[:, 0:512], k.scb3[:, 512:1024], k.vh[:, 0:512]))
            p.comp("vector", lambda e, oc=oc, t0=t0: e.tensor_add(
                k.xc(oc)[:, t0:t0 + 512], k.xc(oc)[:, t0:t0 + 512],
                k.sc32[:, 0:512]))


def layer2_attn(p, k, dr, s_dram):
    SH = [0, -32, -16, -4, -1, 1, 4, 16, 32]
    for oc in range(8):
        wload(p, k, dr["L2_attn_value_WT"], (oc * 128, (oc + 1) * 128), 0, 8)
        for tbb in range(2):
            linear_oc(p, k, k.xnc,
                      lambda kc: k.wb[:, kc * 128:(kc + 1) * 128],
                      oc, 8, tbb * 1024, (tbb + 1) * 1024, k.pm_a)
            p.comp("vector", lambda e, oc=oc, tbb=tbb: e.tensor_copy(
                out=k.pool[:, oc * L + tbb * 1024:oc * L + (tbb + 1) * 1024],
                in_=k.pm_a[:, 0:1024]))
    p.comp("vector", lambda e: e.memset(k.rowa[0:1, 0:4096], 0.0))
    for dc in range(8):
        p.comp("vector", lambda e: e.memset(k.sc32[:], 0.0))
        for n in range(9):
            gc = n * 8 + dc
            wload(p, k, dr["L2_attn_gate_WT"], (gc * 128, (gc + 1) * 128), 0, 8)
            for tbb in range(2):
                linear_oc(p, k, k.xnc,
                          lambda kc: k.wb[:, kc * 128:(kc + 1) * 128],
                          gc, 8, tbb * 1024, (tbb + 1) * 1024,
                          k.pm_a if tbb == 0 else k.pm_b)
                p.comp("scalar", lambda e, tbb=tbb: e.activation(
                    out=k.scb1[:, tbb * 1024:(tbb + 1) * 1024],
                    in_=(k.pm_a if tbb == 0 else k.pm_b)[:, 0:1024],
                    func=AF.Sigmoid))
            s = SH[n]
            if s == 0:
                pieces = [(0, L, 0)]
            elif s > 0:
                pieces = [(0, L - s, s), (L - s, L, s - L)]
            else:
                pieces = [(0, -s, s + L), (-s, L, s)]
            vsl = k.pool[:, dc * L:(dc + 1) * L]

            def fmul(e, pieces=pieces, vsl=vsl):
                last = None
                for (a, b, off) in pieces:
                    for half in range(2):
                        pa, pb = max(a, half * 1024), min(b, (half + 1) * 1024)
                        if pb <= pa:
                            continue
                        pm = k.pm_a if half == 0 else k.pm_b
                        o = half * 1024
                        last = e.tensor_mul(pm[:, pa - o:pb - o],
                                            k.scb1[:, pa:pb],
                                            vsl[:, pa + off:pb + off])
                return last
            p.comp("vector", fmul)

            def fadd(e, pieces=pieces):
                last = None
                for (a, b, off) in pieces:
                    for half in range(2):
                        pa, pb = max(a, half * 1024), min(b, (half + 1) * 1024)
                        if pb <= pa:
                            continue
                        pm = k.pm_a if half == 0 else k.pm_b
                        o = half * 1024
                        last = e.tensor_add(k.sc32[:, pa:pb], k.sc32[:, pa:pb],
                                            pm[:, pa - o:pb - o])
                return last
            p.comp("vector", fadd, force_wait=True)
        p.dma("gpsimd", lambda e, dc=dc: [e.dma_start(
            out=s_dram[dc * 128:(dc + 1) * 128, :], in_=k.sc32[:])])

        def fsum(e, dc=dc):
            last = None
            for sub in range(4):
                pm = (k.pm_a, k.pm_a, k.pm_b, k.pm_b)[sub]
                o = (0, 512, 0, 512)[sub]
                last = e.matmul(pm[0:1, o:o + 512], k.onesf[:, 0:1],
                                k.sc32[:, sub * 512:(sub + 1) * 512],
                                start=True, stop=True)
            return last
        p.comp("tensor", fsum)

        def fsacc(e):
            last = None
            for sub in range(4):
                pm = (k.pm_a, k.pm_a, k.pm_b, k.pm_b)[sub]
                o = (0, 512, 0, 512)[sub]
                last = e.tensor_add(k.rowa[0:1, sub * 512:(sub + 1) * 512],
                                    k.rowa[0:1, sub * 512:(sub + 1) * 512],
                                    pm[0:1, o:o + 512])
            return last
        p.comp("vector", fsacc)
        p.comp("scalar", lambda e: e.activation(out=k.sc32[:], in_=k.sc32[:],
                                                func=AF.Square))

        def fsq(e):
            last = None
            for sub in range(4):
                pm = (k.pm_c, k.pm_d, k.pm_e, k.pm_c)[sub]
                if sub == 3:
                    pass
                last = e.matmul((k.pm_c, k.pm_d, k.pm_e, k.pm_a)[sub][0:1, 0:512],
                                k.onesf[:, 0:1],
                                k.sc32[:, sub * 512:(sub + 1) * 512],
                                start=True, stop=True)
            return last
        p.comp("tensor", fsq)

        def fsqacc(e):
            last = None
            for sub in range(4):
                pm = (k.pm_c, k.pm_d, k.pm_e, k.pm_a)[sub]
                last = e.tensor_add(
                    k.rowa[0:1, 2048 + sub * 512:2048 + (sub + 1) * 512],
                    k.rowa[0:1, 2048 + sub * 512:2048 + (sub + 1) * 512],
                    pm[0:1, 0:512])
            return last
        p.comp("vector", fsqacc)

    vseq(p, [
        lambda e: (e.tensor_scalar_mul(k.rowa[0:1, 0:2048],
                                       k.rowa[0:1, 0:2048], 1.0 / D),
                   e.tensor_scalar_mul(k.rowa[0:1, 2048:4096],
                                       k.rowa[0:1, 2048:4096], 1.0 / D))[-1],
        lambda e: e.tensor_mul(k.sc32[0:1, 0:2048], k.rowa[0:1, 0:2048],
                               k.rowa[0:1, 0:2048]),
        lambda e: e.tensor_sub(k.rowa[0:1, 2048:4096],
                               k.rowa[0:1, 2048:4096],
                               k.sc32[0:1, 0:2048])])
    p.comp("scalar", lambda e: e.activation(out=k.rowa[0:1, 2048:4096],
                                            in_=k.rowa[0:1, 2048:4096],
                                            func=AF.Sqrt,
                                            bias=k.small[0:1, 127:128]))
    p.comp("vector", lambda e: e.reciprocal(k.rowa[0:1, 2048:4096],
                                            k.rowa[0:1, 2048:4096]))
    for tb in range(4):
        t0 = tb * 512

        def fbc(e, t0=t0):
            e.matmul(k.pm_c[:, 0:512], k.onesrow[:], k.rowa[0:1, t0:t0 + 512],
                     start=True, stop=True)
            return e.matmul(k.pm_d[:, 0:512], k.onesrow[:],
                            k.rowa[0:1, 2048 + t0:2048 + t0 + 512],
                            start=True, stop=True)
        p.comp("tensor", fbc)

        def fev(e):
            e.tensor_copy(out=k.sc32[:, 0:512], in_=k.pm_c[:, 0:512])
            return e.tensor_copy(out=k.sc32[:, 512:1024], in_=k.pm_d[:, 0:512])
        p.comp("vector", fev)
        for dc in range(8):
            p.dma("sync", lambda e, dc=dc, t0=t0: [e.dma_start(
                out=k.scb1[:, 0:512],
                in_=s_dram[dc * 128:(dc + 1) * 128, t0:t0 + 512])])

            vseq(p, [
                lambda e: e.tensor_copy(out=k.sc32[:, 1024:1536],
                                        in_=k.scb1[:, 0:512]),
                lambda e: e.tensor_sub(k.sc32[:, 1024:1536],
                                       k.sc32[:, 1024:1536],
                                       k.sc32[:, 0:512]),
                lambda e: e.tensor_mul(k.sc32[:, 1024:1536],
                                       k.sc32[:, 1024:1536],
                                       k.sc32[:, 512:1024])])
            p.comp("vector", lambda e, dc=dc: e.tensor_scalar(
                k.scb1[:, 0:512], k.sc32[:, 1024:1536],
                k.bias[:, 0 + dc:1 + dc], k.bias[:, 8 + dc:9 + dc],
                ALU.mult, ALU.add))
            p.dma("gpsimd", lambda e, dc=dc, t0=t0: [e.dma_start(
                out=s_dram[dc * 128:(dc + 1) * 128, t0:t0 + 512],
                in_=k.scb1[:, 0:512])])


def layer2_ffn(p, k, dr, s_dram):
    for oc in range(8):
        wload(p, k, dr["L2_ffn_weightT"], (oc * 128, (oc + 1) * 128), 0, 8)
        p.comp("scalar", lambda e: e.sign(out=k.wb[:, 1024:2048],
                                          in_=k.wb[:, 0:1024]))
        for tbb in range(2):
            linear_oc(p, k, k.xnc,
                      lambda kc: k.wb[:, 1024 + kc * 128:1024 + (kc + 1) * 128],
                      oc, 8, tbb * 1024, (tbb + 1) * 1024, k.pm_a)
            p.dma("sync", lambda e, oc=oc, tbb=tbb: [e.dma_start(
                out=k.scb2[:, 0:1024],
                in_=s_dram[oc * 128:(oc + 1) * 128,
                           tbb * 1024:(tbb + 1) * 1024])])
            p.comp("scalar", lambda e: e.activation(
                out=k.scb1[:, 0:1024], in_=k.pm_a[:, 0:1024], func=AF.Silu))

            p.comp("vector", lambda e: e.tensor_mul(
                k.sc32[:, 0:1024], k.scb2[:, 0:1024], k.scb1[:, 0:1024]))
            p.comp("vector", lambda e, oc=oc, tbb=tbb: e.tensor_add(
                k.xc(oc)[:, tbb * 1024:(tbb + 1) * 1024],
                k.xc(oc)[:, tbb * 1024:(tbb + 1) * 1024],
                k.sc32[:, 0:1024]))


def layer3_ffn(p, k, dr, s_dram):
    TBW = 256
    for tb in range(8):
        t0 = tb * TBW
        for hc in range(32):
            wload(p, k, dr["L3_ffn_w1_wT"], (hc * 128, (hc + 1) * 128), 0, 8)
            linear_oc(p, k, k.xnc,
                      lambda kc: k.wb[:, kc * 128:(kc + 1) * 128],
                      hc, 8, t0, t0 + TBW, k.pm_c)
            p.comp("scalar", lambda e, hc=hc: e.activation(
                out=k.vh[:, 0:TBW], in_=k.pm_c[:, 0:TBW], func=AF.Silu,
                scale=k.bias[:, 64 + hc:65 + hc]))
            wload(p, k, dr["L3_ffn_w2_wT"], (hc * 128, (hc + 1) * 128), 0, 8)
            linear_oc(p, k, k.xnc,
                      lambda kc: k.wb[:, kc * 128:(kc + 1) * 128],
                      hc, 8, t0, t0 + TBW, k.pm_c)
            p.comp("vector", lambda e, hc=hc: e.tensor_scalar(
                k.vh[:, TBW:2 * TBW], k.pm_c[:, 0:TBW],
                k.bias[:, 96 + hc:97 + hc], None, ALU.mult))
            p.comp("vector", lambda e, hc=hc: e.tensor_mul(
                k.pool[:, hc * TBW:(hc + 1) * TBW], k.vh[:, 0:TBW],
                k.vh[:, TBW:2 * TBW]))
        for hc in range(32):
            mm_streamed(p, k, dr["L3_ffn_h1_wT"], (hc * 128, (hc + 1) * 128),
                        32, lambda kc: k.pool[:, kc * TBW:(kc + 1) * TBW],
                        0, TBW, k.pm_c)
            p.comp("scalar", lambda e, hc=hc: e.activation(
                out=k.pool[:, 8192 + hc * TBW:8192 + (hc + 1) * TBW],
                in_=k.pm_c[:, 0:TBW], func=AF.Gelu,
                scale=k.bias[:, 128 + hc:129 + hc]))
        for hc in range(32):
            mm_streamed(p, k, dr["L3_ffn_h2_wT"], (hc * 128, (hc + 1) * 128),
                        32, lambda kc: k.pool[:, 8192 + kc * TBW:
                                              8192 + (kc + 1) * TBW],
                        0, TBW, k.pm_c)
            p.comp("scalar", lambda e, hc=hc: e.activation(
                out=k.pool[:, hc * TBW:(hc + 1) * TBW],
                in_=k.pm_c[:, 0:TBW], func=AF.Gelu,
                scale=k.bias[:, 160 + hc:161 + hc]))
        for oc in range(8):
            mm_streamed(p, k, dr["L3_ffn_w3_wT"], (oc * 128, (oc + 1) * 128),
                        32, lambda kc: k.pool[:, kc * TBW:(kc + 1) * TBW],
                        0, TBW, k.pm_c)
            p.dma("sync", lambda e, oc=oc, t0=t0: [e.dma_start(
                out=k.vh[:, 512:512 + TBW],
                in_=s_dram[oc * 128:(oc + 1) * 128, t0:t0 + TBW])])
            p.comp("scalar", lambda e, oc=oc: e.activation(
                out=k.vh[:, 0:TBW], in_=k.pm_c[:, 0:TBW], func=AF.Silu,
                scale=k.bias[:, 192 + oc:193 + oc]))

            p.comp("vector", lambda e: e.tensor_mul(
                k.sc32[:, 0:TBW], k.vh[:, 512:512 + TBW], k.vh[:, 0:TBW]))
            p.comp("vector", lambda e, oc=oc, t0=t0: e.tensor_add(
                k.xc(oc)[:, t0:t0 + TBW], k.xc(oc)[:, t0:t0 + TBW],
                k.sc32[:, 0:TBW]))


def dma_list(p, eng, pairs):
    p.dma(eng, lambda e: [e.dma_start(out=o, in_=i) for (o, i) in pairs])


def load_layer_bias(p, k, dr, layer):
    if layer == 0:
        dma_list(p, "sync", [
            (k.cons[:, 0:512], dr["c_locband"][:]),
            (k.bias[:, 0:8], dr["L0_proj_b_c"][:]),
            (k.bias[:, 32:48], dr["L0_spw_t"][:]),
            (k.bias[:, 48:64], dr["L0_spb_t"][:]),
            (k.bias[:, 64:320], dr["L0_b1t"][:]),
            (k.bias[:, 320:384], dr["L0_b2t"][:]),
            (k.bias[:, 384:416], dr["L0_ob1t"][:]),
            (k.bias[:, 416:424], dr["L0_ob2t"][:]),
        ])
    elif layer == 1:
        dma_list(p, "sync", [
            (k.cons[:, 0:512], dr["c_gauss"][:]),
            (k.bias[:, 0:8], dr["L1_bo_c"][:]),
            (k.bias[:, 62:63], dr["L1_lam_t"][:]),
            (k.bias[:, 64:320], dr["L1_b1t"][:]),
            (k.bias[:, 320:384], dr["L1_b2t"][:]),
            (k.hb[:, 0:16], dr["L1_bqh"][:]),
            (k.hb[:, 16:32], dr["L1_bkh"][:]),
            (k.rbuf[:, 0:64], dr["L1_Wg_t"][:]),
            (k.rbuf[:, 64:72], dr["L1_bg_b"][:]),
        ])
    elif layer == 2:
        dma_list(p, "sync", [
            (k.bias[:, 0:8], dr["L2_gamma_c"][:]),
            (k.bias[:, 8:16], dr["L2_beta_c"][:]),
        ])
    else:
        dma_list(p, "gpsimd", [(k.consb[:, :], dr["c_rope"][:])])
        dma_list(p, "sync", [
            (k.bias[:, 0:8], dr["L3_lng_c"][:]),
            (k.bias[:, 8:16], dr["L3_lnb_c"][:]),
            (k.bias[:, 16:24], dr["L3_bout_c"][:]),
            (k.bias[:, 64:96], dr["L3_w1s_c"][:]),
            (k.bias[:, 96:128], dr["L3_w2s_c"][:]),
            (k.bias[:, 128:160], dr["L3_h1s_c"][:]),
            (k.bias[:, 160:192], dr["L3_h2s_c"][:]),
            (k.bias[:, 192:200], dr["L3_w3s_c"][:]),
        ])


def build_layer(p, k, dr, layer, attnout, s_dram):
    load_layer_bias(p, k, dr, layer)
    if layer == 0:
        layernorm(p, k)
        pattern_path(p, k, dr)
        attention_l0(p, k, dr, attnout)
        attn_proj_s(p, k, attnout, dr["L0_attn_projT"], k.bias[:, 0:8],
                    s_dram)
        build_masks_l0(p, k, dr)
        moe_ffn(p, k, dr, 0, s_dram)
    elif layer == 1:
        rdict = {"w": k.rbuf[:, 0:64], "bias": k.rbuf[:, 64:72],
                 "out": k.rbuf[:, 72:200]}
        layernorm(p, k, router=rdict)
        argmax_to_ohT(p, k, k.rbuf[:, 72:200], k.r8[0:8, 0:2048])
        attention_l1(p, k, dr, attnout)
        attn_proj_s(p, k, attnout, dr["L1_attn_WoT"], k.bias[:, 0:8],
                    s_dram)
        moe_ffn(p, k, dr, 1, s_dram)
    elif layer == 2:
        layernorm(p, k)
        layer2_attn(p, k, dr, s_dram)
        layer2_ffn(p, k, dr, s_dram)
    else:
        layernorm(p, k)
        layernorm(p, k, src=k.xnc, dst=k.poolc, src_bf16=True,
                  gamma=lambda dc: k.bias[:, 0 + dc:1 + dc],
                  beta=lambda dc: k.bias[:, 8 + dc:9 + dc])
        attention_l3(p, k, dr, attnout, k.poolc)
        attn_proj_s(p, k, attnout, dr["L3_attn_w_out"], k.bias[:, 16:24],
                    s_dram)
        layer3_ffn(p, k, dr, s_dram)


def declare_inputs(nc, arrs):
    dr = {}
    for name, a in arrs.items():
        dr[name] = nc.dram_tensor(name, list(a.shape), F32,
                                  kind="ExternalInput").ap()
    return dr


def build_layer_prog(arrs, layer):
    nc = bass.Bass("TRN2", target_bir_lowering=False,
                   detect_race_conditions=False,
                   dynamic_dma_scratch_size=8192)
    dr = declare_inputs(nc, arrs)
    dr["x_out"] = nc.dram_tensor("x_out", [D, L], F32,
                                 kind="ExternalOutput").ap()
    attnout = nc.dram_tensor("attnout", [D, L], BF16, kind="Internal").ap()
    s_dram = nc.dram_tensor("s_scr", [D, L], BF16, kind="Internal").ap()
    k = Ker(nc)
    with nc.Block() as blk:
        p = Prog(nc, blk)
        load_globals(p, k, dr)
        load_x(p, k, dr, "x_in")
        build_layer(p, k, dr, layer, attnout, s_dram)
        store_x(p, k, dr, "x_out")
        p.finish()
    return nc


def build_router_prog(arrs):
    nc = bass.Bass("TRN2", target_bir_lowering=False,
                   detect_race_conditions=False,
                   dynamic_dma_scratch_size=8192)
    dr = declare_inputs(nc, arrs)
    dr["r_ohT"] = nc.dram_tensor("r_ohT", [8, L], F32,
                                 kind="ExternalOutput").ap()
    dr["r_cum"] = nc.dram_tensor("r_cum", [8, L], F32,
                                 kind="ExternalOutput").ap()
    dr["r_counts"] = nc.dram_tensor("r_counts", [8, 1], F32,
                                    kind="ExternalOutput").ap()
    k = Ker(nc)
    with nc.Block() as blk:
        p = Prog(nc, blk)
        load_globals(p, k, dr)
        load_x(p, k, dr, "x_in")
        dma_list(p, "sync", [(k.rbuf[:, 0:64], dr["L0_rw_t"][:]),
                             (k.rbuf[:, 64:72], dr["L0_rb_b"][:])])
        rdict = {"w": k.rbuf[:, 0:64], "bias": k.rbuf[:, 64:72],
                 "out": k.rbuf[:, 72:200]}
        layernorm(p, k, router=rdict)
        argmax_to_ohT(p, k, k.rbuf[:, 72:200], k.r8[0:8, 0:2048])
        p.comp("vector", lambda e: e.tensor_tensor_scan(
            k.sc32[0:8, 0:2048], k.r8[0:8, 0:2048], k.r8[0:8, 0:2048], 0.0,
            ALU.add, ALU.bypass))
        dma_list(p, "gpsimd", [(dr["r_ohT"][:], k.r8[0:8, 0:2048]),
                               (dr["r_cum"][:], k.sc32[0:8, 0:2048]),
                               (dr["r_counts"][:], k.sc32[0:8, 2047:2048])])
        p.finish()
    return nc


def host_weights(inp):
    t = lambda a: np.ascontiguousarray(np.asarray(a).T.astype(np.float32))
    til = lambda a: np.ascontiguousarray(
        np.tile(np.asarray(a).astype(np.float32).reshape(1, -1), (128, 1)))
    row = lambda a: np.ascontiguousarray(
        np.asarray(a).astype(np.float32).reshape(1, -1))
    w = dict(host_constants())
    w["L0_attn_qkv_wT"] = t(inp["L0_attn_qkv_w"])
    w["L0_attn_projT"] = t(inp["L0_attn_proj_w"])
    w["L0_proj_b_c"] = chunk_bias(inp["L0_attn_proj_b"])
    featT = t(inp["L0_attn_feat_w"])
    featP = np.zeros((1280, 1024), np.float32)
    featP[0:1024] = featT[0:1024]
    featP[1024] = featT[1024]
    featP[1152] = featT[1025]
    w["L0_attn_featT"] = featP
    w["L0_attn_ps1T"] = t(inp["L0_attn_ps1_w"])
    w["L0_attn_ps2T"] = t(inp["L0_attn_ps2_w"])
    w["L0_attn_ps3T"] = t(inp["L0_attn_ps3_w"])
    w["L0_featb_r"] = row(inp["L0_attn_feat_b"])
    w["L0_ps1b_r"] = row(inp["L0_attn_ps1_b"])
    w["L0_ps2b_r"] = row(inp["L0_attn_ps2_b"])
    w["L0_ps3b_r"] = row(inp["L0_attn_ps3_b"])
    w["L0_patb_r"] = row(inp["L0_attn_pattern_bias"])
    w["L0_spw_t"] = til(inp["L0_attn_sparse_w"])
    w["L0_spb_t"] = til(inp["L0_attn_sparse_b"])
    w["L0_rw_t"] = np.ascontiguousarray(
        np.asarray(inp["L0_ffn_router_w"]).astype(np.float32)
        .reshape(8, 128, 8).transpose(1, 0, 2).reshape(128, 64))
    w["L0_rb_b"] = til(inp["L0_ffn_router_b"])
    w["L0_ffn_w1"] = np.asarray(inp["L0_ffn_w1"]).astype(
        np.float32).reshape(8 * 1024, 4096)
    w["L0_b1t"] = np.ascontiguousarray(
        np.asarray(inp["L0_ffn_b1"]).astype(np.float32)
        .reshape(8, 32, 128).transpose(2, 1, 0).reshape(128, 256))
    w["L0_ffn_w2"] = np.asarray(inp["L0_ffn_w2"]).astype(
        np.float32).reshape(8 * 4096, 1024)
    w["L0_b2t"] = np.ascontiguousarray(
        np.asarray(inp["L0_ffn_b2"]).astype(np.float32)
        .reshape(8, 8, 128).transpose(2, 1, 0).reshape(128, 64))
    w["L0_ffn_ow1"] = np.asarray(inp["L0_ffn_ow1"]).astype(np.float32)
    w["L0_ob1t"] = chunk_bias(inp["L0_ffn_ob1"])
    w["L0_ffn_ow2"] = np.asarray(inp["L0_ffn_ow2"]).astype(np.float32)
    w["L0_ob2t"] = chunk_bias(inp["L0_ffn_ob2"])
    w["L1_attn_WqT"] = t(inp["L1_attn_Wq"])
    w["L1_attn_WkT"] = t(inp["L1_attn_Wk"])
    w["L1_attn_WvT"] = t(inp["L1_attn_Wv"])
    w["L1_attn_WoT"] = t(inp["L1_attn_Wo"])
    w["L1_bqh"] = head_bias(inp["L1_attn_bq"])
    w["L1_bkh"] = head_bias(inp["L1_attn_bk"])
    w["L1_attn_bvt"] = til(inp["L1_attn_bv"])
    w["L1_bo_c"] = chunk_bias(inp["L1_attn_bo"])
    w["L1_lam_t"] = np.full((128, 1), float(np.asarray(
        inp["L1_attn_prior_lambda"])), np.float32)
    w["L1_Wg_t"] = np.ascontiguousarray(
        np.asarray(inp["L1_ffn_Wg"]).astype(np.float32)
        .reshape(8, 128, 8).transpose(1, 0, 2).reshape(128, 64))
    w["L1_bg_b"] = til(inp["L1_ffn_bg"])
    w["L1_ffn_W1"] = np.asarray(inp["L1_ffn_W1"]).astype(
        np.float32).reshape(8 * 1024, 4096)
    w["L1_b1t"] = np.ascontiguousarray(
        np.asarray(inp["L1_ffn_b1"]).astype(np.float32)
        .reshape(8, 32, 128).transpose(2, 1, 0).reshape(128, 256))
    w["L1_ffn_W2"] = np.asarray(inp["L1_ffn_W2"]).astype(
        np.float32).reshape(8 * 4096, 1024)
    w["L1_b2t"] = np.ascontiguousarray(
        np.asarray(inp["L1_ffn_b2"]).astype(np.float32)
        .reshape(8, 8, 128).transpose(2, 1, 0).reshape(128, 64))
    w["L2_attn_gate_WT"] = t(inp["L2_attn_gate_W"])
    w["L2_attn_value_WT"] = t(inp["L2_attn_value_W"])
    w["L2_gamma_c"] = chunk_bias(inp["L2_attn_gamma"])
    w["L2_beta_c"] = chunk_bias(inp["L2_attn_beta"])
    w["L2_ffn_weightT"] = t(inp["L2_ffn_weight"])
    w["L3_lng_c"] = chunk_bias(inp["L3_attn_ln_gamma"])
    w["L3_lnb_c"] = chunk_bias(inp["L3_attn_ln_beta"])
    w["L3_attn_w_qkv"] = np.asarray(inp["L3_attn_w_qkv"]).astype(np.float32)
    w["L3_attn_w_out"] = np.asarray(inp["L3_attn_w_out"]).astype(np.float32)
    w["L3_bout_c"] = chunk_bias(inp["L3_attn_b_out"])
    w["L3_ffn_w1_wT"] = t(inp["L3_ffn_w1_w"])
    w["L3_ffn_w2_wT"] = t(inp["L3_ffn_w2_w"])
    w["L3_ffn_h1_wT"] = t(inp["L3_ffn_h1_w"])
    w["L3_ffn_h2_wT"] = t(inp["L3_ffn_h2_w"])
    w["L3_ffn_w3_wT"] = t(inp["L3_ffn_w3_w"])
    w["L3_w1s_c"] = chunk_bias(inp["L3_ffn_w1_s"])
    w["L3_w2s_c"] = chunk_bias(inp["L3_ffn_w2_s"])
    w["L3_h1s_c"] = chunk_bias(inp["L3_ffn_h1_s"])
    w["L3_h2s_c"] = chunk_bias(inp["L3_ffn_h2_s"])
    w["L3_w3s_c"] = chunk_bias(inp["L3_ffn_w3_s"])
    return w


_CACHE = {}

LAYER_KEYS = {
    0: ["L0_attn_qkv_wT", "L0_attn_projT", "L0_proj_b_c", "L0_attn_featT",
        "L0_attn_ps1T", "L0_attn_ps2T", "L0_attn_ps3T", "L0_featb_r",
        "L0_ps1b_r", "L0_ps2b_r", "L0_ps3b_r", "L0_patb_r", "L0_spw_t",
        "L0_spb_t", "L0_ffn_w1", "L0_b1t", "L0_ffn_w2", "L0_b2t",
        "L0_ffn_ow1", "L0_ob1t", "L0_ffn_ow2", "L0_ob2t",
        "r_ohT", "r_cum", "counts_prior"],
    1: ["L1_attn_WqT", "L1_attn_WkT", "L1_attn_WvT", "L1_attn_WoT",
        "L1_bqh", "L1_bkh", "L1_attn_bvt", "L1_bo_c", "L1_lam_t",
        "L1_Wg_t", "L1_bg_b", "L1_ffn_W1", "L1_b1t", "L1_ffn_W2", "L1_b2t"],
    2: ["L2_attn_gate_WT", "L2_attn_value_WT", "L2_gamma_c", "L2_beta_c",
        "L2_ffn_weightT"],
    3: ["L3_lng_c", "L3_lnb_c", "L3_attn_w_qkv", "L3_attn_w_out", "L3_bout_c",
        "L3_ffn_w1_wT", "L3_ffn_w2_wT", "L3_ffn_h1_wT", "L3_ffn_h2_wT",
        "L3_ffn_w3_wT", "L3_w1s_c", "L3_w2s_c", "L3_h1s_c", "L3_h2s_c",
        "L3_w3s_c"],
}
CONST_KEYS = ["c_identity", "c_gauss", "c_locband", "c_rope", "c_iota8", "c_iota8m",
              "c_ones1", "c_onesrow", "c_pow2"]


def kernel(**inputs):
    w = host_weights(inputs)
    x = np.asarray(inputs["L0_attn_x"]).astype(np.float32)
    xT = [np.ascontiguousarray(x[b].T) for b in range(8)]

    rkeys = CONST_KEYS + ["L0_rw_t", "L0_rb_b"]
    rarr = {kk: w[kk] for kk in rkeys}
    rarr["x_in"] = xT[0]
    if "router" not in _CACHE:
        _CACHE["router"] = build_router_prog(rarr)
    in_maps = []
    for b in range(8):
        m = dict(rarr)
        m["x_in"] = xT[b]
        in_maps.append(m)
    res = run_bass_kernel_spmd(_CACHE["router"], in_maps,
                               core_ids=list(range(8)))
    ohT = [r["r_ohT"].reshape(8, L) for r in res.results]
    cum = [r["r_cum"].reshape(8, L) for r in res.results]
    counts = [r["r_counts"].reshape(8, 1) for r in res.results]

    xcur = xT
    for layer in range(4):
        keys = CONST_KEYS + [kk for kk in LAYER_KEYS[layer]
                             if kk not in ("r_ohT", "r_cum", "counts_prior")]
        arrs = {kk: w[kk] for kk in keys}
        arrs["x_in"] = xcur[0]
        if layer == 0:
            arrs["r_ohT"] = ohT[0]
            arrs["r_cum"] = cum[0]
            cp = np.zeros((8, 8), np.float32)
            arrs["counts_prior"] = cp
        ck = f"layer{layer}"
        if ck not in _CACHE:
            _CACHE[ck] = build_layer_prog(arrs, layer)
        in_maps = []
        for b in range(8):
            m = dict(arrs)
            m["x_in"] = xcur[b]
            if layer == 0:
                m["r_ohT"] = ohT[b]
                m["r_cum"] = cum[b]
                cp = np.zeros((8, 8), np.float32)
                for j in range(b):
                    cp[:, j] = counts[j][:, 0]
                m["counts_prior"] = cp
            in_maps.append(m)
        res = run_bass_kernel_spmd(_CACHE[ck], in_maps,
                                   core_ids=list(range(8)))
        xcur = [np.ascontiguousarray(r["x_out"].reshape(D, L))
                for r in res.results]

    out = np.stack([xc.T for xc in xcur], axis=0)
    return out.astype(np.float32)
